# version 03 (final)
# baseline (speedup 1.0000x reference)
import contextlib
import numpy as np
import concourse.bass as bass
import concourse.mybir as mybir
from concourse.bass_core_rust import IndirectOffsetOnAxis
from concourse.bass_utils import run_bass_kernel_spmd

F32 = mybir.dt.float32
BF16 = mybir.dt.bfloat16
I32 = mybir.dt.int32
AF = mybir.ActivationFunctionType
ALU = mybir.AluOpType
AX = mybir.AxisListType

B, N, D = 4, 4096, 2048
T = 2048
H, DH = 16, 128
BLK = 256
NBLK = T // BLK
E, I_MOE, J_SH = 8, 1408, 5632
C_LOC = 1024
NCH = D // 128
TT = 512
SQT = 256
NSUB = T // SQT
EPS_LN = 1e-5
EPS_RMS = 1e-6


class Sems:

    def __init__(self, nc, es):
        self.nc, self.es = nc, es
        self.m = {}
        self.pool = {}
        self.dead = set()

    _SWDGE = ("p0_wm_ld", "gmisc_ld", "p0_mr_w", "p0_mr_r", "p1b_wld", "p1b_tmw",
              "p2_zst", "p3_wld", "p4a_sent", "p4a_sc", "p4b_wld", "p4c_ugld",
              "p4c_tid", "p4c_gth", "p4c_wdld", "p4d_wld", "p5_g1", "p5_g2")
    _HWDGE = ("c_ld", "misc_ld", "p1a_x", "p1b_st", "p1b_tmst", "p2_pld", "p2_ld",
              "p3_zld", "p3_xld", "p3_xpst", "p3_xprld", "p3_h2tst", "p4b_hhst",
              "p4c_yst", "p4d_hld", "p4d_yst", "p5_yld", "p5_xld", "p5_ost")

    @classmethod
    def _kind(cls, name):
        for p in cls._SWDGE:
            if name.startswith(p):
                return "swdge"
        for p in cls._HWDGE:
            if name.startswith(p):
                return "hwdge"
        return "compute"

    def h(self, name):
        assert name not in self.dead, f"semaphore {name} used after retire"
        if name not in self.m:
            k = self._kind(name)
            pool = self.pool.setdefault(k, [])
            if pool:
                ph, base = pool.pop()
                self.m[name] = [ph, 0, base]
            else:
                self.m[name] = [self.es.enter_context(self.nc.semaphore(name)), 0, 0]
        return self.m[name][0]

    def inc(self, name, n=1):
        h = self.h(name)
        self.m[name][1] += n
        return h, n

    def count(self, name):
        self.h(name)
        return self.m[name][1]

    def wait(self, eng, name, target):
        h = self.h(name)
        if target > 0:
            eng.wait_ge(h, target + self.m[name][2])

    def dinc(self, name, slot, lanes, n=16):
        return self.inc(f"{name}{slot % lanes}", n)

    def dwait(self, eng, name, slot, lanes, per=16):
        if slot >= 0:
            self.wait(eng, f"{name}{slot % lanes}", per * (slot // lanes + 1))

    def dwait_all(self, eng, name, lanes):
        for L in range(lanes):
            nm = f"{name}{L}"
            if nm in self.m:
                self.wait(eng, nm, self.m[nm][1])

    def retire(self, prefixes, keep=()):
        for name in list(self.m):
            if any(name.startswith(p) for p in prefixes) and name not in keep:
                h, cnt, base = self.m.pop(name)
                self.pool.setdefault(self._kind(name), []).append((h, cnt + base))
                self.dead.add(name)


DEBUG_TAPS = False


def build_program():
    nc = bass.Bass("TRN2", target_bir_lowering=False, detect_race_conditions=False)
    _kind_int = "ExternalOutput" if DEBUG_TAPS else "Internal"

    def din(name, shape, dtype=F32):
        return nc.dram_tensor(name, shape, dtype, kind="ExternalInput").ap()

    x_own = din("x_own", [NCH, 128, T])
    x_pri = din("x_pri", [NCH, 128, T])
    c_t = din("c_t", [D, 4])
    wmod_st = din("wmod_st", [NCH, 128, 6 * D])
    slope_row = din("slope_row", [1, H])
    normw_st = din("normw_st", [128, NCH])
    ident_f = din("ident_f", [128, 128])
    su_mat = din("su_mat", [128, 128])
    seed_mask = din("seed_mask", [128, 1])
    wq_st = din("wq_st", [16, D, 128])
    wk_st = din("wk_st", [16, D, 128])
    wg_st = din("wg_st", [16, D, 128])
    wo_st = din("wo_st", [16, D, 128])
    wv_tm = din("wv_tm", [NCH, 128, D])
    wk_tm = din("wk_tm", [NCH, 128, D])
    router_st = din("router_st", [128, NCH, E])
    ug_st = din("ug_st", [E, 22, D, 128])
    wd_st = din("wd_st", [E, I_MOE, D])
    su_st = din("su_st", [88, D, 128])
    wsd_st = din("wsd_st", [4, J_SH, 512])

    y_out = nc.dram_tensor("y_out", [NCH, 128, T], F32, kind="ExternalOutput").ap()

    qf_dram = nc.dram_tensor("qf_dram", [NBLK, H, 128, BLK], BF16, kind=_kind_int).ap()
    kf_dram = nc.dram_tensor("kf_dram", [NBLK, H, 128, BLK], BF16, kind=_kind_int).ap()
    vt_dram = nc.dram_tensor("vt_dram", [NBLK, BLK, D], BF16, kind=_kind_int).ap()
    ktp_dram = nc.dram_tensor("ktp_dram", [NBLK, BLK, D], BF16, kind=_kind_int).ap()
    vtp_dram = nc.dram_tensor("vtp_dram", [NBLK, BLK, D], BF16, kind=_kind_int).ap()
    gate_dram = nc.dram_tensor("gate_dram", [NBLK, NCH, 128, BLK], BF16, kind=_kind_int).ap()
    z_dram = nc.dram_tensor("z_dram", [NBLK, NCH, 128, BLK], BF16, kind=_kind_int).ap()
    xp_dram = nc.dram_tensor("xp_dram", [NCH, 128, T], F32, kind=_kind_int).ap()
    h2t_dram = nc.dram_tensor("h2t_dram", [T, D], BF16, kind=_kind_int).ap()
    modrow_dram = nc.dram_tensor("modrow_dram", [1, 6 * D], F32, kind=_kind_int).ap()
    tokid_dram = nc.dram_tensor("tokid_dram", [E * C_LOC, 1], I32, kind=_kind_int).ap()
    hh_dram = nc.dram_tensor("hh_dram", [4, 44, 128, 512], BF16, kind=_kind_int).ap()
    ybuf_dram = nc.dram_tensor("ybuf_dram", [E * C_LOC, D], F32, kind=_kind_int).ap()
    ysh_dram = nc.dram_tensor("ysh_dram", [T, D], F32, kind=_kind_int).ap()

    es = contextlib.ExitStack()
    with es:
        sem = Sems(nc, es)
        blk = es.enter_context(nc.Block())
        rg_bc_tok = nc.gpsimd.alloc_register("bc_tok")
        rg_bc_row = nc.gpsimd.alloc_register("bc_row")

        mod_sb = es.enter_context(nc.sbuf_tensor("mod_sb", [128, 96], F32))
        cpool = es.enter_context(nc.sbuf_tensor("cpool", [128, 8], F32))
        ones_col = es.enter_context(nc.sbuf_tensor("ones_col", [128, 1], F32))
        ones_colb = es.enter_context(nc.sbuf_tensor("ones_colb", [128, 1], BF16))
        ones_row = es.enter_context(nc.sbuf_tensor("ones_row", [1, 128], F32))
        normw_sb = es.enter_context(nc.sbuf_tensor("normw_sb", [128, NCH], F32))
        identf_sb = es.enter_context(nc.sbuf_tensor("identf_sb", [128, 128], F32))
        identb_sb = es.enter_context(nc.sbuf_tensor("identb_sb", [128, 128], BF16))
        su_sb = es.enter_context(nc.sbuf_tensor("su_sb", [128, 128], BF16))
        seedm_sb = es.enter_context(nc.sbuf_tensor("seedm_sb", [128, 1], F32))
        iota8 = es.enter_context(nc.sbuf_tensor("iota8", [128, 8], F32))
        iota8p8 = es.enter_context(nc.sbuf_tensor("iota8p8", [128, 8], F32))
        tokid_sb = es.enter_context(nc.sbuf_tensor("tokid_sb", [128, 16], I32))
        negs_sb = es.enter_context(nc.sbuf_tensor("negs_sb", [128, H], F32))
        spos_sb = es.enter_context(nc.sbuf_tensor("spos_sb", [128, H], F32))
        idx1_sb = es.enter_context(nc.sbuf_tensor("idx1_sb", [128, 16], I32))
        idx2_sb = es.enter_context(nc.sbuf_tensor("idx2_sb", [128, 16], I32))
        cf1_sb = es.enter_context(nc.sbuf_tensor("cf1_sb", [128, 16], F32))
        cf2_sb = es.enter_context(nc.sbuf_tensor("cf2_sb", [128, 16], F32))
        logit_sb = es.enter_context(nc.sbuf_tensor("logit_sb", [8, T], F32))
        rmod_sb = es.enter_context(nc.sbuf_tensor("rmod_sb", [128, NCH, E], F32))
        rconst_sb = es.enter_context(nc.sbuf_tensor("rconst_sb", [8, 1], F32))

        CP_EPSLN, CP_EPSRMS, CP_NHALF, CP_INVD, CP_N256 = range(5)

        p0 = contextlib.ExitStack()
        pwm = contextlib.ExitStack()
        c_sb = p0.enter_context(nc.sbuf_tensor("c_sb", [128, 16, 4], F32))
        silu_c = p0.enter_context(nc.sbuf_tensor("silu_c", [128, 16, 4], BF16))
        wm_ring = [pwm.enter_context(nc.sbuf_tensor(f"wm{i}", [128, 3 * D], BF16, side="right"))
                   for i in range(2)]
        modacc = p0.enter_context(nc.sbuf_tensor("modacc", [4, 6 * D], F32))
        slope_sb = p0.enter_context(nc.sbuf_tensor("slope_sb", [1, H], F32))
        ps_m0 = p0.enter_context(nc.psum_tensor("ps_m0", [4, 512], F32))
        ps_m1 = p0.enter_context(nc.psum_tensor("ps_m1", [4, 512], F32))
        ps_bc = p0.enter_context(nc.psum_tensor("ps_bc", [128, H], F32))
        ps_rc = p0.enter_context(nc.psum_tensor("ps_rc", [8, 1], F32))
        ps_mr = [p0.enter_context(nc.psum_tensor(f"ps_mr{i}", [128, 4], F32)) for i in range(2)]

        @blk.sync
        def _(sy):
            sy.dma_start(out=c_sb[:], in_=c_t.rearrange("(k p) b -> p k b", p=128)).then_inc(*sem.inc("c_ld", 16))
            sy.dma_start(out=slope_sb[:], in_=slope_row).then_inc(*sem.inc("misc_ld", 16))
            sy.dma_start(out=normw_sb[:], in_=normw_st).then_inc(*sem.inc("misc_ld", 16))
            sy.dma_start(out=identf_sb[:], in_=ident_f).then_inc(*sem.inc("misc_ld", 16))
            sy.dma_start(out=seedm_sb[:], in_=seed_mask).then_inc(*sem.inc("misc_ld", 16))
            sy.dma_start(out=rmod_sb[:], in_=router_st).then_inc(*sem.inc("misc_ld", 16))

        @blk.vector
        def _(v):
            for i, val in [(CP_EPSLN, EPS_LN), (CP_EPSRMS, EPS_RMS), (CP_NHALF, -0.5),
                           (CP_INVD, 1.0 / D), (CP_N256, -256.0)]:
                v.memset(cpool[:, i:i + 1], float(val))
            v.memset(ones_col[:], 1.0)
            v.memset(ones_colb[:], 1.0)
            v.memset(ones_row[:], 1.0)
            v.memset(modacc[:], 0.0).then_inc(*sem.inc("p0_const", 1))

        @blk.gpsimd
        def _(g):
            g.dma_start(out=identb_sb[:], in_=ident_f).then_inc(*sem.inc("gmisc_ld", 16))
            g.dma_start(out=su_sb[:], in_=su_mat).then_inc(*sem.inc("gmisc_ld", 16))
            g.iota(iota8[:], pattern=[[1, 8]], base=0, channel_multiplier=0,
                   allow_small_or_imprecise_dtypes=True)
            g.iota(iota8p8[:], pattern=[[1, 8]], base=8, channel_multiplier=0,
                   allow_small_or_imprecise_dtypes=True)
            g.iota(tokid_sb[:], pattern=[[128, 16]], base=0, channel_multiplier=1
                   ).then_inc(*sem.inc("p0_iota", 1))
            g.reg_mov(rg_bc_tok, T - 1)
            g.reg_mov(rg_bc_row, E * C_LOC - 1)
            for slab in range(32):
                if slab >= 2:
                    sem.wait(g, "p0_mod_ev", 12 * (slab - 1))
                half, k = slab // NCH, slab % NCH
                g.dma_start(out=wm_ring[slab % 2][:],
                            in_=wmod_st[k, :, half * 3 * D:(half + 1) * 3 * D]
                            ).then_inc(*sem.dinc("p0_wm_ld", slab, 2))
            sem.wait(g, "p0_mod_ev", 384)
            g.dma_start(out=modrow_dram, in_=modacc[0:1, :]).then_inc(*sem.inc("p0_mr_w", 16))

        @blk.scalar
        def _(s):
            sem.wait(s, "c_ld", 16)
            s.activation(out=silu_c[:], in_=c_sb[:], func=AF.Silu).then_inc(*sem.inc("p0_siluc", 1))

        @blk.tensor
        def _(t):
            sem.wait(t, "p0_siluc", 1)
            nm = 0
            for slab in range(32):
                half, k = slab // NCH, slab % NCH
                sem.dwait(t, "p0_wm_ld", slab, 2)
                for n in range(12):
                    ps = ps_m0 if nm % 2 == 0 else ps_m1
                    if nm >= 2:
                        sem.wait(t, "p0_mod_ev", nm - 1)
                    t.matmul(ps[:], silu_c[:, k, :], wm_ring[slab % 2][:, n * 512:(n + 1) * 512],
                             start=True, stop=True).then_inc(*sem.inc("p0_mod_mm", 1))
                    nm += 1
            sem.wait(t, "misc_ld", 80)
            t.matmul(ps_bc[:], ones_row[:], slope_sb[:], start=True, stop=True
                     ).then_inc(*sem.inc("p0_sbc", 1))
            sem.wait(t, "p0_mod_ev", 384)
            for seg in range(96):
                if seg >= 2:
                    sem.wait(t, "p0_mrev", seg - 1)
                t.matmul(ps_mr[seg % 2][:], modacc[:, seg * 128:(seg + 1) * 128],
                         identf_sb[0:4, 0:4], start=True, stop=True
                         ).then_inc(*sem.inc("p0_mrT", 1))

        @blk.vector
        def _(v):
            sem.wait(v, "p0_const", 1)
            nm = 0
            for slab in range(32):
                half = slab // NCH
                for n in range(12):
                    sem.wait(v, "p0_mod_mm", nm + 1)
                    ps = ps_m0 if nm % 2 == 0 else ps_m1
                    col0 = half * 3 * D + n * 512
                    v.tensor_add(modacc[:, col0:col0 + 512], modacc[:, col0:col0 + 512],
                                 ps[:]).then_inc(*sem.inc("p0_mod_ev", 1))
                    nm += 1
            sem.wait(v, "p0_sbc", 1)
            v.tensor_scalar_mul(negs_sb[:], ps_bc[:], -1.0).then_inc(*sem.inc("p0_negs", 1))
            v.tensor_copy(spos_sb[:], ps_bc[:]).then_inc(*sem.inc("p0_negs", 1))

        @blk.vector
        def _(v):
            for seg in range(96):
                sem.wait(v, "p0_mrT", seg + 1)
                v.tensor_copy(mod_sb[:, seg:seg + 1], ps_mr[seg % 2][:, 0:1]
                              ).then_inc(*sem.inc("p0_mrev", 1))
            def _sv(inst):
                inst.then_inc(*sem.inc("vhz", 1))
                sem.wait(v, "vhz", sem.count("vhz"))
            _sv(v.tensor_scalar_add(mod_sb[:, 16:32], mod_sb[:, 16:32], 1.0))
            _sv(v.tensor_scalar_add(mod_sb[:, 64:80], mod_sb[:, 64:80], 1.0))
            for g0 in (32, 80):
                ga = mod_sb[:, g0:g0 + 16]
                _sv(v.tensor_scalar(cf1_sb[:], ga, 0.0, None, ALU.is_ge))
                _sv(v.tensor_scalar(cf1_sb[:], cf1_sb[:], 2.0, -1.0, ALU.mult, ALU.add))
                _sv(v.tensor_scalar_mul(cf2_sb[:], ga, -1.0))
                _sv(v.tensor_tensor(cf2_sb[:], ga, cf2_sb[:], ALU.max))
                _sv(v.tensor_scalar_max(cf2_sb[:], cf2_sb[:], 0.25))
                _sv(v.tensor_tensor(ga, cf1_sb[:], cf2_sb[:], ALU.mult))
            for c in range(NCH):
                _sv(v.tensor_scalar_mul(rmod_sb[:, c, :], rmod_sb[:, c, :], mod_sb[:, 64 + c:65 + c]))
            v.tensor_copy(mod_sb[:, 0:1], mod_sb[:, 0:1]).then_inc(*sem.inc("p0_mod_done", 1))

        @blk.tensor
        def _(t):
            sem.wait(t, "p0_mod_done", 1)
            for c in range(NCH):
                mm = t.matmul(ps_rc[:], rmod_sb[:, c, :], mod_sb[:, 48 + c:49 + c],
                              start=(c == 0), stop=(c == NCH - 1))
            mm.then_inc(*sem.inc("p0_rc_mm", 1))

        @blk.vector
        def _(v):
            sem.wait(v, "p0_rc_mm", 1)
            v.tensor_copy(rconst_sb[:], ps_rc[:]).then_inc(*sem.inc("p0_rc", 1))

        p0.close()
        p1 = contextlib.ExitStack()
        h1_own = p1.enter_context(nc.sbuf_tensor("h1_own", [128, NCH, T], BF16))
        h1_pri = p1.enter_context(nc.sbuf_tensor("h1_pri", [128, NCH, T], BF16))

        p1a = contextlib.ExitStack()
        xt_sb = p1a.enter_context(nc.sbuf_tensor("xt_sb", [128, NCH, SQT], F32))
        sqp = [p1a.enter_context(nc.sbuf_tensor(f"sqp{i}", [128, SQT], BF16)) for i in range(2)]
        xbp = [p1a.enter_context(nc.sbuf_tensor(f"xbp{i}", [128, SQT], BF16)) for i in range(2)]
        tmpp = [p1a.enter_context(nc.sbuf_tensor(f"tmpp{i}", [128, SQT], F32)) for i in range(2)]
        rows_sb = p1a.enter_context(nc.sbuf_tensor("rows_sb", [1, 8, SQT], F32))
        ps_s1 = p1a.enter_context(nc.psum_tensor("ps_s1", [1, SQT], F32))
        ps_s2 = p1a.enter_context(nc.psum_tensor("ps_s2", [1, SQT], F32))
        ps_A = p1a.enter_context(nc.psum_tensor("ps_A", [128, SQT], F32))
        ps_B = p1a.enter_context(nc.psum_tensor("ps_B", [128, SQT], F32))

        for tix in range(2 * NSUB):
            own = tix < NSUB
            st_ = tix % NSUB
            xsrc = x_own if own else x_pri
            h1dst = h1_own if own else h1_pri
            c0, c1 = st_ * SQT, (st_ + 1) * SQT

            @blk.sync
            def _(sy, tix=tix, xsrc=xsrc, c0=c0, c1=c1):
                if tix >= 1:
                    sem.wait(sy, "p1a_pre", 16 * tix)
                for c in range(NCH):
                    sy.dma_start(out=xt_sb[:, c, :], in_=xsrc[c, :, c0:c1]
                                 ).then_inc(*sem.inc("p1a_x", 16))

            @blk.scalar
            def _(s, tix=tix):
                sem.wait(s, "p1a_x", 256 * (tix + 1))
                for c in range(NCH):
                    if tix * NCH + c >= 2:
                        sem.wait(s, "p1a_smm", 2 * (tix * NCH + c) - 2)
                    s.activation(out=sqp[c % 2][:], in_=xt_sb[:, c, :], func=AF.Square)
                    s.activation(out=xbp[c % 2][:], in_=xt_sb[:, c, :], func=AF.Identity
                                 ).then_inc(*sem.inc("p1a_sq", 1))

            @blk.tensor
            def _(t, tix=tix):
                if tix == 0:
                    sem.wait(t, "p0_rc", 1)
                    sem.wait(t, "p0_const", 1)
                else:
                    sem.wait(t, "p1a_var", tix)
                for c in range(NCH):
                    sem.wait(t, "p1a_sq", tix * NCH + c + 1)
                    t.matmul(ps_s1[:], ones_colb[:], xbp[c % 2][:],
                             start=(c == 0), stop=(c == NCH - 1)).then_inc(*sem.inc("p1a_smm", 1))
                    t.matmul(ps_s2[:], ones_colb[:], sqp[c % 2][:],
                             start=(c == 0), stop=(c == NCH - 1)).then_inc(*sem.inc("p1a_smm", 1))
                sem.wait(t, "p1a_rows", tix + 1)
                if tix >= 1:
                    sem.wait(t, "p1a_pre", 16 * tix)
                t.matmul(ps_A[:], ones_row[:], rows_sb[:, 0, :], start=True, stop=True
                         ).then_inc(*sem.inc("p1a_bc", 1))
                t.matmul(ps_B[:], ones_row[:], rows_sb[:, 1, :], start=True, stop=True
                         ).then_inc(*sem.inc("p1a_bc", 1))

            @blk.vector
            def _(v, tix=tix):
                sem.wait(v, "p1a_smm", 32 * (tix + 1))
                v.tensor_scalar_mul(rows_sb[0:1, 2, :], ps_s1[:], 1.0 / D)
                v.tensor_tensor(rows_sb[0:1, 3, :], rows_sb[0:1, 2, :], rows_sb[0:1, 2, :], ALU.mult)
                v.scalar_tensor_tensor(rows_sb[0:1, 4, :], ps_s2[:], 1.0 / D,
                                       rows_sb[0:1, 3, :], ALU.mult, ALU.subtract
                                       ).then_inc(*sem.inc("p1a_var", 1))

            @blk.scalar
            def _(s, tix=tix):
                sem.wait(s, "p1a_var", tix + 1)
                s.activation(out=rows_sb[0:1, 5, :], in_=rows_sb[0:1, 4, :], func=AF.Ln,
                             bias=cpool[0:1, CP_EPSLN:CP_EPSLN + 1])
                s.activation(out=rows_sb[0:1, 0, :], in_=rows_sb[0:1, 5, :], func=AF.Exp,
                             scale=cpool[0:1, CP_NHALF:CP_NHALF + 1]).then_inc(*sem.inc("p1a_rstd", 1))

            @blk.vector
            def _(v, tix=tix):
                sem.wait(v, "p1a_rstd", tix + 1)
                v.tensor_tensor(rows_sb[0:1, 1, :], rows_sb[0:1, 2, :], rows_sb[0:1, 0, :],
                                ALU.mult).then_inc(*sem.inc("p1a_rows", 1))
                sem.wait(v, "p1a_bc", 2 * (tix + 1))
                for c in range(NCH):
                    if tix * NCH + c >= 2:
                        sem.wait(v, "p1a_h1", tix * NCH + c - 1)
                    v.tensor_tensor(tmpp[c % 2][:], xt_sb[:, c, :], ps_A[:], ALU.mult)
                    v.tensor_tensor(tmpp[c % 2][:], tmpp[c % 2][:], ps_B[:], ALU.subtract
                                    ).then_inc(*sem.inc("p1a_pre", 1))

            @blk.scalar
            def _(s, tix=tix, h1dst=h1dst, c0=c0, c1=c1):
                if tix == 0:
                    sem.wait(s, "p0_mod_done", 1)
                for c in range(NCH):
                    sem.wait(s, "p1a_pre", tix * NCH + c + 1)
                    s.activation(out=h1dst[:, c, c0:c1], in_=tmpp[c % 2][:],
                                 func=AF.Identity, scale=mod_sb[:, 16 + c:17 + c],
                                 bias=mod_sb[:, c:c + 1]).then_inc(*sem.inc("p1a_h1", 1))

        p1a.close()
        pwm.close()
        p1b = contextlib.ExitStack()
        wch = [p1b.enter_context(nc.sbuf_tensor(f"wch{i}", [128, 16, 128], BF16))
               for i in range(4)]
        ev_sb = [p1b.enter_context(nc.sbuf_tensor(f"ev{i}", [128, TT], BF16))
                 for i in range(4)]
        ps_p1 = [p1b.enter_context(nc.psum_tensor(f"ps_p1_{i}", [128, TT], F32))
                 for i in range(4)]

        fm_jobs = ([("q", m, wq_st, AF.Silu, qf_dram) for m in range(16)]
                   + [("k", m, wk_st, AF.Silu, kf_dram) for m in range(16)]
                   + [("g", m, wg_st, AF.Sigmoid, gate_dram) for m in range(16)])

        @blk.gpsimd
        def _(g):
            sem.wait(g, "p0_mod_ev", 384)
            sem.wait(g, "p1a_h1", 256)
            for j, (kind, m, wsrc, act, dst) in enumerate(fm_jobs):
                if j >= 4:
                    sem.wait(g, "p1b_mm", 4 * (j - 3))
                g.dma_start(out=wch[j % 4][:], in_=wsrc[m].rearrange("(k p) m -> p k m", p=128)
                            ).then_inc(*sem.dinc("p1b_wld", j, 4))

        @blk.tensor
        def _(t):
            sem.wait(t, "p1a_h1", 256)
            for j, (kind, m, wsrc, act, dst) in enumerate(fm_jobs):
                sem.dwait(t, "p1b_wld", j, 4)
                for tile in range(4):
                    n = 4 * j + tile
                    if n >= 4:
                        sem.wait(t, "p1b_ev", n - 3)
                    for k in range(NCH):
                        mm = t.matmul(ps_p1[n % 4][:], wch[j % 4][:, k, :],
                                      h1_own[:, k, tile * TT:(tile + 1) * TT],
                                      start=(k == 0), stop=(k == NCH - 1))
                    mm.then_inc(*sem.inc("p1b_mm", 1))

        @blk.scalar
        def _(s):
            for j, (kind, m, wsrc, act, dst) in enumerate(fm_jobs):
                for tile in range(4):
                    n = 4 * j + tile
                    sem.wait(s, "p1b_mm", n + 1)
                    if n >= 4:
                        sem.dwait(s, "p1b_st", n - 4, 4, per=32)
                    s.activation(out=ev_sb[n % 4][:], in_=ps_p1[n % 4][:], func=act
                                 ).then_inc(*sem.inc("p1b_ev", 1))

        @blk.sync
        def _(sy):
            for j, (kind, m, wsrc, act, dst) in enumerate(fm_jobs):
                for tile in range(4):
                    n = 4 * j + tile
                    sem.wait(sy, "p1b_ev", n + 1)
                    for half in range(2):
                        sy.dma_start(out=dst[tile * 2 + half, m, :, :],
                                     in_=ev_sb[n % 4][:, half * BLK:(half + 1) * BLK]
                                     ).then_inc(*sem.dinc("p1b_st", n, 4))

        tm_jobs = [("vo", wv_tm, h1_own, vt_dram), ("kp", wk_tm, h1_pri, ktp_dram),
                   ("vp", wv_tm, h1_pri, vtp_dram)]
        wsl = [p1b.enter_context(nc.sbuf_tensor(f"wsl{i}", [128, 16, TT], BF16))
               for i in range(2)]

        @blk.gpsimd
        def _(g):
            for j, (kind, wsrc, h1src, dst) in enumerate(tm_jobs):
                for ns in range(4):
                    q = 4 * j + ns
                    if q >= 2:
                        sem.wait(g, "p1b_tmmm", 16 * (q - 1))
                    for k in range(NCH):
                        g.dma_start(out=wsl[q % 2][:, k, :],
                                    in_=wsrc[k, :, ns * TT:(ns + 1) * TT]
                                    ).then_inc(*sem.dinc("p1b_tmw", q, 2))

        @blk.tensor
        def _(t):
            sem.wait(t, "p1b_ev", 192)
            tot = 0
            for j, (kind, wsrc, h1src, dst) in enumerate(tm_jobs):
                for ns in range(4):
                    q = 4 * j + ns
                    sem.dwait(t, "p1b_tmw", q, 2, per=256)
                    for tc in range(16):
                        if tot >= 4:
                            sem.wait(t, "p1b_tmev", tot - 3)
                        for k in range(NCH):
                            mm = t.matmul(ps_p1[tot % 4][:], h1src[:, k, tc * 128:(tc + 1) * 128],
                                          wsl[q % 2][:, k, :],
                                          start=(k == 0), stop=(k == NCH - 1))
                        mm.then_inc(*sem.inc("p1b_tmmm", 1))
                        tot += 1

        @blk.scalar
        def _(s):
            sem.dwait_all(s, "p1b_st", 4)
            tot = 0
            for _ in range(12):
                for tc in range(16):
                    sem.wait(s, "p1b_tmmm", tot + 1)
                    if tot >= 4:
                        sem.dwait(s, "p1b_tmst", tot - 4, 4)
                    s.activation(out=ev_sb[tot % 4][:], in_=ps_p1[tot % 4][:], func=AF.Silu
                                 ).then_inc(*sem.inc("p1b_tmev", 1))
                    tot += 1

        @blk.sync
        def _(sy):
            tot = 0
            for j, (kind, wsrc, h1src, dst) in enumerate(tm_jobs):
                for ns in range(4):
                    for tc in range(16):
                        sem.wait(sy, "p1b_tmev", tot + 1)
                        sy.dma_start(out=dst[tc // 2, (tc % 2) * 128:(tc % 2 + 1) * 128,
                                             ns * TT:(ns + 1) * TT],
                                     in_=ev_sb[tot % 4][:]).then_inc(*sem.dinc("p1b_tmst", tot, 4))
                        tot += 1

        p1b.close()
        p1.close()
        sem.retire(["p0_", "c_ld", "misc_ld", "p1a_"], keep=("p0_negs",))
        p2 = contextlib.ExitStack()
        qd_sb = p2.enter_context(nc.sbuf_tensor("qd_sb", [128, H, BLK], BF16))
        ddT_sb = p2.enter_context(nc.sbuf_tensor("ddT_sb", [128, H, 2, BLK], BF16))
        kd_sb = p2.enter_context(nc.sbuf_tensor("kd_sb", [128, H, 2], F32))
        bd_sb = p2.enter_context(nc.sbuf_tensor("bd_sb", [128, H], F32))
        dscr = p2.enter_context(nc.sbuf_tensor("dscr", [128, BLK], F32))
        dscr2 = p2.enter_context(nc.sbuf_tensor("dscr2", [128, 2, BLK], F32))
        kdscr = p2.enter_context(nc.sbuf_tensor("kdscr", [128, 2], F32))
        kvf = p2.enter_context(nc.sbuf_tensor("kvf", [128, H, 128], F32))
        kvb = p2.enter_context(nc.sbuf_tensor("kvb", [128, H, 128], BF16))
        qfm = [p2.enter_context(nc.sbuf_tensor(f"qfm{i}", [128, H, BLK], BF16)) for i in range(2)]
        kfm = [p2.enter_context(nc.sbuf_tensor(f"kfm{i}", [128, H, BLK], BF16)) for i in range(2)]
        vtb = [p2.enter_context(nc.sbuf_tensor(f"vtb{i}", [128, 2, D], BF16)) for i in range(2)]
        gtb = [p2.enter_context(nc.sbuf_tensor(f"gtb{i}", [128, NCH, BLK], BF16)) for i in range(2)]
        ktpb = [p2.enter_context(nc.sbuf_tensor(f"ktpb{i}", [128, 2, D], BF16)) for i in range(2)]
        vtpb = [p2.enter_context(nc.sbuf_tensor(f"vtpb{i}", [128, 2, D], BF16)) for i in range(2)]
        ktr = p2.enter_context(nc.sbuf_tensor("ktr", [128, 2, 128], BF16))
        qp_sb = p2.enter_context(nc.sbuf_tensor("qp_sb", [128, BLK], BF16))
        st_sb = p2.enter_context(nc.sbuf_tensor("st_sb", [128, 2, BLK], BF16))
        attn_sb = p2.enter_context(nc.sbuf_tensor("attn_sb", [128, NCH, BLK], BF16))
        z_sb = [p2.enter_context(nc.sbuf_tensor(f"z_sb{i}", [128, NCH, BLK], BF16)) for i in range(2)]
        sq2 = [p2.enter_context(nc.sbuf_tensor(f"sq2_{i}", [128, BLK], BF16)) for i in range(2)]
        tmp2 = p2.enter_context(nc.sbuf_tensor("tmp2", [128, BLK], F32))
        rrow = p2.enter_context(nc.sbuf_tensor("rrow", [1, 2, BLK], F32))
        ps_tr = [p2.enter_context(nc.psum_tensor(f"ps_tr{i}", [128, 128], BF16)) for i in range(2)]
        ps_S = [p2.enter_context(nc.psum_tensor(f"ps_S{i}", [128, BLK], F32)) for i in range(2)]
        ps_O = p2.enter_context(nc.psum_tensor("ps_O", [128, BLK], F32))
        ps_A2 = p2.enter_context(nc.psum_tensor("ps_A2", [128, 128], F32))
        ps_ms = p2.enter_context(nc.psum_tensor("ps_ms", [1, BLK], F32))
        ps_rb = p2.enter_context(nc.psum_tensor("ps_rb", [128, BLK], F32))

        @blk.gpsimd
        def _(g):
            sem.wait(g, "p1b_tmmm", 192)
            g.iota(dscr[:], pattern=[[1, BLK]], base=1, channel_multiplier=0,
                   allow_small_or_imprecise_dtypes=True)
            for jc in range(2):
                g.iota(dscr2[:, jc, :], pattern=[[1, BLK]], base=-128 * jc,
                       channel_multiplier=-1, allow_small_or_imprecise_dtypes=True)
                g.affine_select(dscr2[:, jc, :], dscr2[:, jc, :], pattern=[[1, BLK]],
                                base=-128 * jc, channel_multiplier=-1,
                                compare_op=ALU.is_ge, fill=1e9)
            g.iota(kdscr[:], pattern=[[-128, 2]], base=255, channel_multiplier=-1,
                   allow_small_or_imprecise_dtypes=True).then_inc(*sem.inc("p2_dtab", 1))

        @blk.scalar
        def _(s):
            sem.wait(s, "p2_dtab", 1)
            sem.wait(s, "p0_negs", 2)
            for h in range(H):
                s.activation(out=qd_sb[:, h, :], in_=dscr[:], func=AF.Exp,
                             scale=negs_sb[:, h:h + 1])
                for jc in range(2):
                    s.activation(out=ddT_sb[:, h, jc, :], in_=dscr2[:, jc, :], func=AF.Exp,
                                 scale=negs_sb[:, h:h + 1])
                s.activation(out=kd_sb[:, h, :], in_=kdscr[:], func=AF.Exp,
                             scale=negs_sb[:, h:h + 1])
            s.activation(out=bd_sb[:], in_=spos_sb[:], func=AF.Exp,
                         scale=cpool[:, CP_N256:CP_N256 + 1]).then_inc(*sem.inc("p2_decay", 1))

        @blk.vector
        def _(v):
            sem.wait(v, "p1b_tmmm", 192)
            v.memset(kvf[:], 0.0).then_inc(*sem.inc("p2_kv0", 1))

        @blk.sync
        def _(sy):
            sem.dwait_all(sy, "p1b_tmst", 4)
            for j in range(NBLK):
                if j >= 2:
                    sem.wait(sy, "p2_kvu", 16 * (j - 1))
                sy.dma_start(out=ktpb[j % 2][:], in_=ktp_dram[j].rearrange("(jc p) d -> p jc d", p=128)
                             ).then_inc(*sem.dinc("p2_pld", j, 2))
                sy.dma_start(out=vtpb[j % 2][:], in_=vtp_dram[j].rearrange("(jc p) d -> p jc d", p=128)
                             ).then_inc(*sem.dinc("p2_pld", j, 2))

        for j in range(NBLK):
            @blk.vector
            def _(v, j=j):
                sem.dwait(v, "p2_pld", j, 2, per=32)
                if j == 0:
                    sem.wait(v, "p2_decay", 1)
                    sem.wait(v, "p2_kv0", 1)
                for h in range(H):
                    hg = j * H + h
                    if hg >= 1:
                        sem.wait(v, "p2_pA", hg)
                    v.tensor_scalar_mul(ktr[:, 0, :], ktpb[j % 2][:, 0, h * 128:(h + 1) * 128],
                                        kd_sb[:, h, 0:1]).then_inc(*sem.inc("p2_pktr", 1))
                    v.tensor_scalar_mul(ktr[:, 1, :], ktpb[j % 2][:, 1, h * 128:(h + 1) * 128],
                                        kd_sb[:, h, 1:2]).then_inc(*sem.inc("p2_pktr", 1))
                    sem.wait(v, "p2_pA", hg + 1)
                    v.scalar_tensor_tensor(kvf[:, h, :], kvf[:, h, :], bd_sb[:, h:h + 1],
                                           ps_A2[:], ALU.mult, ALU.add).then_inc(*sem.inc("p2_kvu", 1))

            @blk.tensor
            def _(t, j=j):
                for h in range(H):
                    hg = j * H + h
                    sem.wait(t, "p2_pktr", 2 * hg + 2)
                    if hg >= 1:
                        sem.wait(t, "p2_kvu", hg)
                    for jc in range(2):
                        mm = t.matmul(ps_A2[:], ktr[:, jc, :], vtpb[j % 2][:, jc, h * 128:(h + 1) * 128],
                                      start=(jc == 0), stop=(jc == 1))
                    mm.then_inc(*sem.inc("p2_pA", 1))

        @blk.vector
        def _(v):
            sem.wait(v, "p2_kvu", 128)
            for h in range(H):
                v.tensor_scalar_mul(kvf[:, h, :], kvf[:, h, :], seedm_sb[:, 0:1])
            v.tensor_copy(kvf[:, 0, 0:1], kvf[:, 0, 0:1]).then_inc(*sem.inc("p2_mask", 1))

        @blk.sync
        def _(sy):
            sem.dwait_all(sy, "p1b_st", 4)
            sem.dwait_all(sy, "p1b_tmst", 4)
            for j in range(NBLK):
                if j >= 2:
                    sem.dwait(sy, "p2_zst", j - 2, 2, per=256)
                sy.dma_start(out=qfm[j % 2][:], in_=qf_dram[j].rearrange("h p n -> p h n")
                             ).then_inc(*sem.dinc("p2_ld", j, 2))
                sy.dma_start(out=kfm[j % 2][:], in_=kf_dram[j].rearrange("h p n -> p h n")
                             ).then_inc(*sem.dinc("p2_ld", j, 2))
                sy.dma_start(out=vtb[j % 2][:], in_=vt_dram[j].rearrange("(jc p) d -> p jc d", p=128)
                             ).then_inc(*sem.dinc("p2_ld", j, 2))
                sy.dma_start(out=gtb[j % 2][:], in_=gate_dram[j].rearrange("c p n -> p c n")
                             ).then_inc(*sem.dinc("p2_ld", j, 2))

        for j in range(NBLK):
            @blk.tensor
            def _(t, j=j):
                sem.dwait(t, "p2_ld", j, 2, per=64)
                if j == 0:
                    sem.wait(t, "p2_mask", 1)
                    sem.wait(t, "gmisc_ld", 32)
                for h in range(H):
                    hg = j * H + h
                    if hg >= 1:
                        sem.wait(t, "p2_ktr2", 2 * hg)
                    for jc in range(2):
                        t.transpose(ps_tr[jc][:], kfm[j % 2][:, h, jc * 128:(jc + 1) * 128],
                                    identb_sb[:]).then_inc(*sem.inc("p2_tr", 1))
                    if hg >= 1:
                        sem.wait(t, "p2_sev", 2 * hg)
                    for jc in range(2):
                        t.matmul(ps_S[jc][:], kfm[j % 2][:, h, jc * 128:(jc + 1) * 128],
                                 qfm[j % 2][:, h, :], start=True, stop=True
                                 ).then_inc(*sem.inc("p2_S", 1))
                    sem.wait(t, "p2_qp", hg + 1)
                    sem.wait(t, "p2_kvb", hg + 1)
                    sem.wait(t, "p2_sev", 2 * hg + 2)
                    if hg >= 1:
                        sem.wait(t, "p2_aev", hg)
                    t.matmul(ps_O[:], kvb[:, h, :], qp_sb[:], start=True, stop=False)
                    t.matmul(ps_O[:], vtb[j % 2][:, 0, h * 128:(h + 1) * 128], st_sb[:, 0, :],
                             start=False, stop=False)
                    t.matmul(ps_O[:], vtb[j % 2][:, 1, h * 128:(h + 1) * 128], st_sb[:, 1, :],
                             start=False, stop=True).then_inc(*sem.inc("p2_O", 1))
                    sem.wait(t, "p2_ktr2", 2 * hg + 2)
                    if hg >= 1:
                        sem.wait(t, "p2_kvu2", hg)
                    for jc in range(2):
                        mm = t.matmul(ps_A2[:], ktr[:, jc, :], vtb[j % 2][:, jc, h * 128:(h + 1) * 128],
                                      start=(jc == 0), stop=(jc == 1))
                    mm.then_inc(*sem.inc("p2_A", 1))

            @blk.vector
            def _(v, j=j):
                sem.dwait(v, "p2_ld", j, 2, per=64)
                if j >= 1:
                    sem.wait(v, "p2_sq", 16 * j)
                    sem.wait(v, "p2_zev", 16 * j)
                for h in range(H):
                    hg = j * H + h
                    if hg >= 1:
                        sem.wait(v, "p2_O", hg)
                    v.tensor_copy(kvb[:, h, :], kvf[:, h, :]).then_inc(*sem.inc("p2_kvb", 1))
                    v.tensor_tensor(qp_sb[:], qfm[j % 2][:, h, :], qd_sb[:, h, :], ALU.mult
                                    ).then_inc(*sem.inc("p2_qp", 1))
                    sem.wait(v, "p2_tr", 2 * hg + 2)
                    if hg >= 1:
                        sem.wait(v, "p2_A", hg)
                    v.tensor_scalar_mul(ktr[:, 0, :], ps_tr[0][:], kd_sb[:, h, 0:1]
                                        ).then_inc(*sem.inc("p2_ktr2", 1))
                    v.tensor_scalar_mul(ktr[:, 1, :], ps_tr[1][:], kd_sb[:, h, 1:2]
                                        ).then_inc(*sem.inc("p2_ktr2", 1))
                    sem.wait(v, "p2_S", 2 * hg + 2)
                    if hg >= 1:
                        sem.wait(v, "p2_O", hg)
                    v.tensor_tensor(st_sb[:, 0, :], ps_S[0][:], ddT_sb[:, h, 0, :], ALU.mult
                                    ).then_inc(*sem.inc("p2_sev", 1))
                    v.tensor_tensor(st_sb[:, 1, :], ps_S[1][:], ddT_sb[:, h, 1, :], ALU.mult
                                    ).then_inc(*sem.inc("p2_sev", 1))
                    sem.wait(v, "p2_O", hg + 1)
                    v.tensor_copy(attn_sb[:, h, :], ps_O[:]).then_inc(*sem.inc("p2_aev", 1))
                    sem.wait(v, "p2_A", hg + 1)
                    v.scalar_tensor_tensor(kvf[:, h, :], kvf[:, h, :], bd_sb[:, h:h + 1],
                                           ps_A2[:], ALU.mult, ALU.add).then_inc(*sem.inc("p2_kvu2", 1))

            @blk.scalar
            def _(s, j=j):
                sem.wait(s, "p2_aev", (j + 1) * H)
                for c in range(NCH):
                    n = j * NCH + c
                    if n >= 2:
                        sem.wait(s, "p2_msmm", n - 1)
                    s.activation(out=sq2[c % 2][:], in_=attn_sb[:, c, :], func=AF.Square
                                 ).then_inc(*sem.inc("p2_sq", 1))

            @blk.tensor
            def _(t, j=j):
                if j >= 1:
                    sem.wait(t, "p2_rstd", j)
                for c in range(NCH):
                    sem.wait(t, "p2_sq", j * NCH + c + 1)
                    t.matmul(ps_ms[:], ones_colb[:], sq2[c % 2][:],
                             start=(c == 0), stop=(c == NCH - 1)).then_inc(*sem.inc("p2_msmm", 1))
                sem.wait(t, "p2_rstd", j + 1)
                if j >= 1:
                    sem.wait(t, "p2_zpre", 16 * j)
                t.matmul(ps_rb[:], ones_row[:], rrow[:, 1, :], start=True, stop=True
                         ).then_inc(*sem.inc("p2_rbc", 1))

            @blk.scalar
            def _(s, j=j):
                sem.wait(s, "p2_msmm", (j + 1) * NCH)
                if j >= 1:
                    sem.wait(s, "p2_rbc", j)
                s.activation(out=rrow[0:1, 0, :], in_=ps_ms[:], func=AF.Ln,
                             scale=cpool[0:1, CP_INVD:CP_INVD + 1],
                             bias=cpool[0:1, CP_EPSRMS:CP_EPSRMS + 1])
                s.activation(out=rrow[0:1, 1, :], in_=rrow[0:1, 0, :], func=AF.Exp,
                             scale=cpool[0:1, CP_NHALF:CP_NHALF + 1]).then_inc(*sem.inc("p2_rstd", 1))

            @blk.vector
            def _(v, j=j):
                sem.wait(v, "p2_rbc", j + 1)
                if j >= 2:
                    sem.dwait(v, "p2_zst", j - 2, 2, per=256)
                for c in range(NCH):
                    v.tensor_tensor(tmp2[:], attn_sb[:, c, :], ps_rb[:], ALU.mult
                                    ).then_inc(*sem.inc("p2_zpre", 1))
                    v.tensor_tensor(tmp2[:], tmp2[:], gtb[j % 2][:, c, :], ALU.mult)
                    v.tensor_scalar_mul(z_sb[j % 2][:, c, :], tmp2[:], normw_sb[:, c:c + 1]
                                        ).then_inc(*sem.inc("p2_zev", 1))

            @blk.gpsimd
            def _(g, j=j):
                for c in range(NCH):
                    sem.wait(g, "p2_zev", j * NCH + c + 1)
                    g.dma_start(out=z_dram[j, c, :, :], in_=z_sb[j % 2][:, c, :]
                                ).then_inc(*sem.dinc("p2_zst", j, 2))


        p2.close()
        p3o = contextlib.ExitStack()
        p3a_s = contextlib.ExitStack()
        ph2 = contextlib.ExitStack()
        srows = p3o.enter_context(nc.sbuf_tensor("srows", [1, 8, TT], F32))
        rows3 = p3o.enter_context(nc.sbuf_tensor("rows3", [1, 8, TT], F32))
        zall = p3a_s.enter_context(nc.sbuf_tensor("zall", [128, NCH, T], BF16))
        h2_sb = ph2.enter_context(nc.sbuf_tensor("h2_sb", [128, NCH, T], BF16, side="right"))
        wch3 = [p3a_s.enter_context(nc.sbuf_tensor(f"wch3_{i}", [128, 16, 128], BF16))
                for i in range(2)]
        xch = [p3a_s.enter_context(nc.sbuf_tensor(f"xch{i}", [128, TT], F32)) for i in range(2)]
        xpev = [p3a_s.enter_context(nc.sbuf_tensor(f"xpev{i}", [128, TT], F32)) for i in range(2)]
        sq3 = [p3a_s.enter_context(nc.sbuf_tensor(f"sq3_{i}", [128, TT], BF16)) for i in range(2)]
        xb3 = [p3a_s.enter_context(nc.sbuf_tensor(f"xb3_{i}", [128, TT], BF16)) for i in range(2)]
        p3aps = contextlib.ExitStack()
        ps_w3 = [p3aps.enter_context(nc.psum_tensor(f"ps_w3_{i}", [128, TT], F32)) for i in range(2)]
        ps_st1 = p3aps.enter_context(nc.psum_tensor("ps_st1", [1, TT], F32))
        ps_st2 = p3aps.enter_context(nc.psum_tensor("ps_st2", [1, TT], F32))

        if DEBUG_TAPS:
            zdbg = nc.dram_tensor("zdbg", [128, NCH, T], BF16, kind="ExternalOutput").ap()
            psdbg = nc.dram_tensor("psdbg", [128, TT], F32, kind="ExternalOutput").ap()
            psdbg_sb = p3a_s.enter_context(nc.sbuf_tensor("psdbg_sb", [128, TT], F32))

            xpdbg = nc.dram_tensor("xpdbg", [128, TT], F32, kind="ExternalOutput").ap()
            xpdbg_sb = p3a_s.enter_context(nc.sbuf_tensor("xpdbg_sb", [128, TT], F32))
            msdbg = nc.dram_tensor("msdbg", [128, 96], F32, kind="ExternalOutput").ap()
            msdbg_sb = p3a_s.enter_context(nc.sbuf_tensor("msdbg_sb", [128, 96], F32))

            @blk.vector
            def _(v):
                sem.wait(v, "p3_mm", 1)
                v.tensor_copy(psdbg_sb[:], ps_w3[0][:])
                v.tensor_copy(msdbg_sb[:], mod_sb[:]).then_inc(*sem.inc("dbg_ps", 1))



            @blk.gpsimd
            def _(g):
                sem.wait(g, "p3_zld", 16 * NBLK)
                g.dma_start(out=zdbg, in_=zall[:]).then_inc(*sem.inc("dbg_st", 16))

        @blk.sync
        def _(sy):
            sem.dwait_all(sy, "p2_zst", 2)
            for j in range(NBLK):
                sy.dma_start(out=zall[:, :, j * BLK:(j + 1) * BLK],
                             in_=z_dram[j].rearrange("c p n -> p c n")
                             ).then_inc(*sem.inc("p3_zld", 16))

        @blk.gpsimd
        def _(g):
            sem.dwait_all(g, "p2_zst", 2)
            for m in range(16):
                if m >= 2:
                    sem.wait(g, "p3_mm", 4 * (m - 1))
                g.dma_start(out=wch3[m % 2][:], in_=wo_st[m].rearrange("(k p) m -> p k m", p=128)
                            ).then_inc(*sem.dinc("p3_wld", m, 2))

        if DEBUG_TAPS:
            @blk.gpsimd
            def _(g):
                sem.wait(g, "dbg_ps", 2)
                g.dma_start(out=psdbg, in_=psdbg_sb[:]).then_inc(*sem.inc("dbg_st", 16))
                g.dma_start(out=xpdbg, in_=xpdbg_sb[:]).then_inc(*sem.inc("dbg_st", 16))
                g.dma_start(out=msdbg, in_=msdbg_sb[:]).then_inc(*sem.inc("dbg_st", 16))

        @blk.tensor
        def _(t):
            sem.wait(t, "p3_zld", 16 * NBLK)
            for m in range(16):
                sem.dwait(t, "p3_wld", m, 2)
                for tile in range(4):
                    n = 4 * m + tile
                    if n >= 2:
                        sem.wait(t, "p3_xpev", n - 1)
                    for k in range(NCH):
                        mm = t.matmul(ps_w3[n % 2][:], wch3[m % 2][:, k, :],
                                      zall[:, k, tile * TT:(tile + 1) * TT],
                                      start=(k == 0), stop=(k == NCH - 1))
                    mm.then_inc(*sem.inc("p3_mm", 1))
                    if n >= 2:
                        nlag = n - 2
                        sem.wait(t, "p3_sq", nlag + 1)
                        if nlag >= 1:
                            sem.wait(t, "p3_srows", nlag)
                        t.matmul(ps_st1[:], ones_colb[:], xb3[nlag % 2][:], start=True, stop=True
                                 ).then_inc(*sem.inc("p3_stmm", 1))
                        t.matmul(ps_st2[:], ones_colb[:], sq3[nlag % 2][:], start=True, stop=True
                                 ).then_inc(*sem.inc("p3_stmm", 1))
            for nlag in (62, 63):
                sem.wait(t, "p3_sq", nlag + 1)
                sem.wait(t, "p3_srows", nlag)
                t.matmul(ps_st1[:], ones_colb[:], xb3[nlag % 2][:], start=True, stop=True
                         ).then_inc(*sem.inc("p3_stmm", 1))
                t.matmul(ps_st2[:], ones_colb[:], sq3[nlag % 2][:], start=True, stop=True
                         ).then_inc(*sem.inc("p3_stmm", 1))

        @blk.sync
        def _(sy):
            for m in range(16):
                for tile in range(4):
                    n = 4 * m + tile
                    if n >= 2:
                        sem.wait(sy, "p3_xpev", n - 1)
                    sy.dma_start(out=xch[n % 2][:], in_=x_own[m, :, tile * TT:(tile + 1) * TT]
                                 ).then_inc(*sem.dinc("p3_xld", n, 2))
                    if n >= 1:
                        sem.wait(sy, "p3_sq", n)
                        mprev, tprev = (n - 1) // 4, (n - 1) % 4
                        sy.dma_start(out=xp_dram[mprev, :, tprev * TT:(tprev + 1) * TT],
                                     in_=xpev[(n - 1) % 2][:]).then_inc(*sem.dinc("p3_xpst", n - 1, 2))
            sem.wait(sy, "p3_sq", 64)
            sy.dma_start(out=xp_dram[15, :, 3 * TT:4 * TT], in_=xpev[63 % 2][:]
                         ).then_inc(*sem.dinc("p3_xpst", 63, 2))

        @blk.vector
        def _(v):
            sem.dwait_all(v, "p2_zst", 2)
            v.memset(srows[:], 0.0).then_inc(*sem.inc("p3_sr0", 1))
            for m in range(16):
                for tile in range(4):
                    n = 4 * m + tile
                    sem.wait(v, "p3_mm", n + 1)
                    sem.dwait(v, "p3_xld", n, 2)
                    if n >= 2:
                        sem.dwait(v, "p3_xpst", n - 2, 2)
                    v.scalar_tensor_tensor(xpev[n % 2][:], ps_w3[n % 2][:],
                                           mod_sb[:, 32 + m:33 + m], xch[n % 2][:],
                                           ALU.mult, ALU.add).then_inc(*sem.inc("p3_xpev", 1))
                    if DEBUG_TAPS and n == 0:
                        v.tensor_copy(xpdbg_sb[:], xpev[0][:]).then_inc(*sem.inc("dbg_ps", 1))
                    if n >= 2:
                        nlag = n - 2
                        sem.wait(v, "p3_stmm", 2 * (nlag + 1))
                        tl = nlag % 4
                        v.tensor_add(srows[0:1, tl, :], srows[0:1, tl, :], ps_st1[:])
                        v.tensor_add(srows[0:1, 4 + tl, :], srows[0:1, 4 + tl, :], ps_st2[:]
                                     ).then_inc(*sem.inc("p3_srows", 1))
            for nlag in (62, 63):
                sem.wait(v, "p3_stmm", 2 * (nlag + 1))
                tl = nlag % 4
                v.tensor_add(srows[0:1, tl, :], srows[0:1, tl, :], ps_st1[:])
                v.tensor_add(srows[0:1, 4 + tl, :], srows[0:1, 4 + tl, :], ps_st2[:]
                             ).then_inc(*sem.inc("p3_srows", 1))

        @blk.scalar
        def _(s):
            for m in range(16):
                for tile in range(4):
                    n = 4 * m + tile
                    sem.wait(s, "p3_xpev", n + 1)
                    if n >= 2:
                        sem.wait(s, "p3_stmm", 2 * (n - 1))
                    s.activation(out=sq3[n % 2][:], in_=xpev[n % 2][:], func=AF.Square)
                    s.activation(out=xb3[n % 2][:], in_=xpev[n % 2][:], func=AF.Identity
                                 ).then_inc(*sem.inc("p3_sq", 1))

        p3aps.close()
        p3a_s.close()
        p3b = contextlib.ExitStack()
        xpr = [p3b.enter_context(nc.sbuf_tensor(f"xpr{i}", [128, TT], F32)) for i in range(2)]
        h2pre = [p3b.enter_context(nc.sbuf_tensor(f"h2pre{i}", [128, TT], F32)) for i in range(2)]
        h2tr = [p3b.enter_context(nc.sbuf_tensor(f"h2tr{i}", [128, D], BF16)) for i in range(4)]
        ps_A3 = p3b.enter_context(nc.psum_tensor("ps_A3", [128, TT], F32))
        ps_B3 = p3b.enter_context(nc.psum_tensor("ps_B3", [128, TT], F32))
        ps_lg = p3b.enter_context(nc.psum_tensor("ps_lg", [8, TT], F32))
        ps_t3 = [p3b.enter_context(nc.psum_tensor(f"ps_t3_{i}", [128, 128], BF16)) for i in range(2)]

        for tile in range(4):
            @blk.vector
            def _(v, tile=tile):
                sem.wait(v, "p3_srows", 64)
                if tile >= 1:
                    sem.wait(v, "p3_lgev", tile)
                v.tensor_scalar_mul(rows3[0:1, 2, :], srows[0:1, tile, :], 1.0 / D)
                v.tensor_tensor(rows3[0:1, 3, :], rows3[0:1, 2, :], rows3[0:1, 2, :], ALU.mult)
                v.scalar_tensor_tensor(rows3[0:1, 4, :], srows[0:1, 4 + tile, :], 1.0 / D,
                                       rows3[0:1, 3, :], ALU.mult, ALU.subtract
                                       ).then_inc(*sem.inc("p3_var2", 1))

            @blk.scalar
            def _(s, tile=tile):
                sem.wait(s, "p3_var2", tile + 1)
                s.activation(out=rows3[0:1, 5, :], in_=rows3[0:1, 4, :], func=AF.Ln,
                             bias=cpool[0:1, CP_EPSLN:CP_EPSLN + 1])
                s.activation(out=rows3[0:1, 0, :], in_=rows3[0:1, 5, :], func=AF.Exp,
                             scale=cpool[0:1, CP_NHALF:CP_NHALF + 1]).then_inc(*sem.inc("p3_rstd2", 1))

            @blk.vector
            def _(v, tile=tile):
                sem.wait(v, "p3_rstd2", tile + 1)
                v.tensor_tensor(rows3[0:1, 1, :], rows3[0:1, 2, :], rows3[0:1, 0, :], ALU.mult
                                ).then_inc(*sem.inc("p3_rows2", 1))

            @blk.tensor
            def _(t, tile=tile):
                sem.wait(t, "p3_rows2", tile + 1)
                if tile >= 1:
                    sem.wait(t, "p3_preev", 16 * tile)
                t.matmul(ps_A3[:], ones_row[:], rows3[:, 0, :], start=True, stop=True)
                t.matmul(ps_B3[:], ones_row[:], rows3[:, 1, :], start=True, stop=True
                         ).then_inc(*sem.inc("p3_bc2", 1))

            @blk.sync
            def _(sy, tile=tile):
                if tile == 0:
                    sem.dwait_all(sy, "p3_xpst", 2)
                for c in range(NCH):
                    n3 = tile * NCH + c
                    if n3 >= 2:
                        sem.wait(sy, "p3_preev", n3 - 1)
                    sy.dma_start(out=xpr[c % 2][:], in_=xp_dram[c, :, tile * TT:(tile + 1) * TT]
                                 ).then_inc(*sem.dinc("p3_xprld", n3, 2))

            @blk.vector
            def _(v, tile=tile):
                sem.wait(v, "p3_bc2", tile + 1)
                for c in range(NCH):
                    n3 = tile * NCH + c
                    sem.dwait(v, "p3_xprld", n3, 2)
                    if n3 >= 2:
                        sem.wait(v, "p3_h2", n3 - 1)
                        sem.wait(v, "p3_lgmm", n3 - 1)
                    v.tensor_tensor(h2pre[c % 2][:], xpr[c % 2][:], ps_A3[:], ALU.mult)
                    v.tensor_tensor(h2pre[c % 2][:], h2pre[c % 2][:], ps_B3[:], ALU.subtract
                                    ).then_inc(*sem.inc("p3_preev", 1))

            @blk.scalar
            def _(s, tile=tile):
                for c in range(NCH):
                    n3 = tile * NCH + c
                    sem.wait(s, "p3_preev", n3 + 1)
                    s.activation(out=h2_sb[:, c, tile * TT:(tile + 1) * TT], in_=h2pre[c % 2][:],
                                 func=AF.Identity, scale=mod_sb[:, 64 + c:65 + c],
                                 bias=mod_sb[:, 48 + c:49 + c]).then_inc(*sem.inc("p3_h2", 1))

            @blk.tensor
            def _(t, tile=tile):
                for c in range(NCH):
                    n3 = tile * NCH + c
                    sem.wait(t, "p3_preev", n3 + 1)
                    t.matmul(ps_lg[:], rmod_sb[:, c, :], h2pre[c % 2][:],
                             start=(c == 0), stop=(c == NCH - 1)).then_inc(*sem.inc("p3_lgmm", 1))
                for c in range(NCH):
                    sem.wait(t, "p3_h2", tile * NCH + c + 1)
                    for i in range(4):
                        nt = (tile * NCH + c) * 4 + i
                        if nt >= 2:
                            sem.wait(t, "p3_trev", nt - 1)
                        t.transpose(ps_t3[nt % 2][:],
                                    h2_sb[:, c, tile * TT + i * 128: tile * TT + (i + 1) * 128],
                                    identb_sb[:]).then_inc(*sem.inc("p3_tr", 1))

            @blk.vector
            def _(v, tile=tile):
                if tile >= 1:
                    for i4 in range(4):
                        sem.wait(v, f"p3_h2tst{i4}", 16 * tile)
                for c in range(NCH):
                    for i in range(4):
                        nt = (tile * NCH + c) * 4 + i
                        sem.wait(v, "p3_tr", nt + 1)
                        v.tensor_copy(h2tr[i][:, c * 128:(c + 1) * 128], ps_t3[nt % 2][:]
                                      ).then_inc(*sem.inc("p3_trev", 1))
                sem.wait(v, "p3_lgmm", (tile + 1) * NCH)
                v.tensor_scalar(logit_sb[:, tile * TT:(tile + 1) * TT], ps_lg[:],
                                rconst_sb[:, 0:1], None, ALU.add).then_inc(*sem.inc("p3_lgev", 1))

            @blk.sync
            def _(sy, tile=tile):
                for i in range(4):
                    sem.wait(sy, "p3_trev", tile * 64 + 61 + i)
                    sy.dma_start(out=h2t_dram[tile * TT + i * 128: tile * TT + (i + 1) * 128, :],
                                 in_=h2tr[i][:]).then_inc(*sem.inc(f"p3_h2tst{i}", 16))

        p3b.close()
        p3o.close()
        sem.retire(["p1b_", "p2_", "p0_negs"])
        p4a = contextlib.ExitStack()
        ltm = p4a.enter_context(nc.sbuf_tensor("ltm", [128, 8], F32))
        m1c = p4a.enter_context(nc.sbuf_tensor("m1c", [128, 1], F32))
        m2c = p4a.enter_context(nc.sbuf_tensor("m2c", [128, 1], F32))
        eqc = p4a.enter_context(nc.sbuf_tensor("eqc", [128, 8], F32))
        cdc = p4a.enter_context(nc.sbuf_tensor("cdc", [128, 8], F32))
        am1c = p4a.enter_context(nc.sbuf_tensor("am1c", [128, 1], F32))
        am2c = p4a.enter_context(nc.sbuf_tensor("am2c", [128, 1], F32))
        oh1c = p4a.enter_context(nc.sbuf_tensor("oh1c", [128, 8], F32))
        oh2c = p4a.enter_context(nc.sbuf_tensor("oh2c", [128, 8], F32))
        lmv = p4a.enter_context(nc.sbuf_tensor("lmv", [128, 8], F32))
        w1c = p4a.enter_context(nc.sbuf_tensor("w1c", [128, 1], F32))
        w2c = p4a.enter_context(nc.sbuf_tensor("w2c", [128, 1], F32))
        dmc = p4a.enter_context(nc.sbuf_tensor("dmc", [128, 1], F32))
        Xc = p4a.enter_context(nc.sbuf_tensor("Xc", [128, 8], F32))
        Xb = p4a.enter_context(nc.sbuf_tensor("Xb", [128, 8], BF16))
        posc = p4a.enter_context(nc.sbuf_tensor("posc", [128, 8], F32))
        rb_sb = p4a.enter_context(nc.sbuf_tensor("rb_sb", [128, 8], F32))
        tot_sb = p4a.enter_context(nc.sbuf_tensor("tot_sb", [1, 8], F32))
        t8 = p4a.enter_context(nc.sbuf_tensor("t8", [128, 8], F32))
        p1c = p4a.enter_context(nc.sbuf_tensor("p1c", [128, 1], F32))
        p2c_ = p4a.enter_context(nc.sbuf_tensor("p2c_", [128, 1], F32))
        ov1c = p4a.enter_context(nc.sbuf_tensor("ov1c", [128, 1], F32))
        ov2c = p4a.enter_context(nc.sbuf_tensor("ov2c", [128, 1], F32))
        if1c = p4a.enter_context(nc.sbuf_tensor("if1c", [128, 1], F32))
        if2c = p4a.enter_context(nc.sbuf_tensor("if2c", [128, 1], F32))
        sentz = p4a.enter_context(nc.sbuf_tensor("sentz", [128, 64], I32))
        ps_l4 = p4a.enter_context(nc.psum_tensor("ps_l4", [128, 8], F32))
        ps_pos = p4a.enter_context(nc.psum_tensor("ps_pos", [128, 8], F32))
        ps_tot = p4a.enter_context(nc.psum_tensor("ps_tot", [1, 8], F32))
        ps_rbd = p4a.enter_context(nc.psum_tensor("ps_rbd", [128, 8], F32))

        @blk.vector
        def _(v):
            sem.dwait_all(v, "p3_h2tst", 4)
            v.memset(rb_sb[:], 0.0)
            v.memset(sentz[:], 4096)
            v.tensor_copy(sentz[:, 0:1], sentz[:, 0:1]).then_inc(*sem.inc("p4a_init", 1))

        @blk.gpsimd
        def _(g):
            sem.wait(g, "p4a_init", 1)
            g.dma_start(out=tokid_dram.rearrange("(p c) one -> p (c one)", p=128), in_=sentz[:]
                        ).then_inc(*sem.inc("p4a_sent", 16))

        for tc in range(16):
            @blk.tensor
            def _(t, tc=tc):
                if tc == 0:
                    sem.wait(t, "p3_lgev", 4)
                    sem.wait(t, "gmisc_ld", 32)
                else:
                    sem.wait(t, "p4a_ltm", tc)
                t.matmul(ps_l4[:], logit_sb[:, tc * 128:(tc + 1) * 128], identf_sb[0:8, 0:8],
                         start=True, stop=True).then_inc(*sem.inc("p4a_lmm", 1))
                sem.wait(t, "p4a_Xb", tc + 1)
                if tc >= 1:
                    sem.wait(t, "p4a_pose", tc)
                    sem.wait(t, "p4a_rbup", tc)
                t.matmul(ps_pos[:], su_sb[:], Xb[:], start=True, stop=True
                         ).then_inc(*sem.inc("p4a_posmm", 1))
                t.matmul(ps_tot[:], ones_col[:], Xc[:], start=True, stop=True
                         ).then_inc(*sem.inc("p4a_totmm", 1))
                sem.wait(t, "p4a_tote", tc + 1)
                t.matmul(ps_rbd[:], ones_row[:], tot_sb[:], start=True, stop=True
                         ).then_inc(*sem.inc("p4a_rbdmm", 1))

            @blk.vector
            def _(v, tc=tc):
                def _sv(inst):
                    inst.then_inc(*sem.inc("vhz", 1))
                    sem.wait(v, "vhz", sem.count("vhz"))
                sem.wait(v, "p4a_lmm", tc + 1)
                v.tensor_copy(ltm[:], ps_l4[:]).then_inc(*sem.inc("p4a_ltm", 1))
                sem.wait(v, "p4a_ltm", sem.count("p4a_ltm"))
                _sv(v.tensor_reduce(m1c[:], ltm[:], axis=AX.X, op=ALU.max))
                _sv(v.tensor_scalar(eqc[:], ltm[:], m1c[:, 0:1], None, ALU.is_equal))
                _sv(v.scalar_tensor_tensor(cdc[:], eqc[:], -8.0, iota8p8[:], ALU.mult, ALU.add))
                _sv(v.tensor_reduce(am1c[:], cdc[:], axis=AX.X, op=ALU.min))
                _sv(v.tensor_scalar(oh1c[:], iota8[:], am1c[:, 0:1], None, ALU.is_equal))
                _sv(v.scalar_tensor_tensor(lmv[:], oh1c[:], -1e9, ltm[:], ALU.mult, ALU.add))
                _sv(v.tensor_reduce(m2c[:], lmv[:], axis=AX.X, op=ALU.max))
                _sv(v.tensor_scalar(eqc[:], lmv[:], m2c[:, 0:1], None, ALU.is_equal))
                _sv(v.scalar_tensor_tensor(cdc[:], eqc[:], -8.0, iota8p8[:], ALU.mult, ALU.add))
                _sv(v.tensor_reduce(am2c[:], cdc[:], axis=AX.X, op=ALU.min))
                _sv(v.tensor_scalar(oh2c[:], iota8[:], am2c[:, 0:1], None, ALU.is_equal))
                if tc >= 1:
                    sem.wait(v, "p4a_w1", tc)
                v.tensor_sub(dmc[:], m1c[:], m2c[:]).then_inc(*sem.inc("p4a_dm", 1))
                _zz_p4a_dm = sem.wait(v, "p4a_dm", sem.count("p4a_dm"))
                if tc >= 1:
                    sem.wait(v, "p4a_totmm", tc)
                    sem.wait(v, "p4a_posmm", tc)
                _sv(v.tensor_add(Xc[:], oh1c[:], oh2c[:]))
                v.tensor_copy(Xb[:], Xc[:]).then_inc(*sem.inc("p4a_Xb", 1))
                _zz_p4a_Xb = sem.wait(v, "p4a_Xb", sem.count("p4a_Xb"))
                sem.wait(v, "p4a_posmm", tc + 1)
                v.tensor_copy(posc[:], ps_pos[:]).then_inc(*sem.inc("p4a_pose", 1))
                _zz_p4a_pose = sem.wait(v, "p4a_pose", sem.count("p4a_pose"))
                _sv(v.tensor_add(posc[:], posc[:], rb_sb[:]))
                sem.wait(v, "p4a_totmm", tc + 1)
                v.tensor_copy(tot_sb[:], ps_tot[:]).then_inc(*sem.inc("p4a_tote", 1))
                _zz_p4a_tote = sem.wait(v, "p4a_tote", sem.count("p4a_tote"))
                sem.wait(v, "p4a_rbdmm", tc + 1)
                v.tensor_add(rb_sb[:], rb_sb[:], ps_rbd[:]).then_inc(*sem.inc("p4a_rbup", 1))
                _zz_p4a_rbup = sem.wait(v, "p4a_rbup", sem.count("p4a_rbup"))
                _sv(v.tensor_tensor(t8[:], posc[:], oh1c[:], ALU.mult))
                _sv(v.tensor_reduce(p1c[:], t8[:], axis=AX.X, op=ALU.add))
                _sv(v.tensor_tensor(t8[:], posc[:], oh2c[:], ALU.mult))
                _sv(v.tensor_reduce(p2c_[:], t8[:], axis=AX.X, op=ALU.add))
                _sv(v.tensor_scalar(ov1c[:], p1c[:], float(C_LOC), None, ALU.is_ge))
                _sv(v.tensor_scalar(ov2c[:], p2c_[:], float(C_LOC), None, ALU.is_ge))
                _sv(v.scalar_tensor_tensor(if1c[:], am1c[:], float(C_LOC), p1c[:], ALU.mult, ALU.add))
                _sv(v.scalar_tensor_tensor(if1c[:], ov1c[:], 16384.0, if1c[:], ALU.mult, ALU.add))
                _sv(v.scalar_tensor_tensor(if2c[:], am2c[:], float(C_LOC), p2c_[:], ALU.mult, ALU.add))
                _sv(v.scalar_tensor_tensor(if2c[:], ov2c[:], 16384.0, if2c[:], ALU.mult, ALU.add))
                _sv(v.tensor_copy(idx1_sb[:, tc:tc + 1], if1c[:]))
                v.tensor_copy(idx2_sb[:, tc:tc + 1], if2c[:]).then_inc(*sem.inc("p4a_idx", 1))
                _zz_p4a_idx = sem.wait(v, "p4a_idx", sem.count("p4a_idx"))

            @blk.scalar
            def _(s, tc=tc):
                sem.wait(s, "p4a_dm", tc + 1)
                if tc >= 1:
                    sem.wait(s, "p4a_cf", tc)
                s.activation(out=w1c[:], in_=dmc[:], func=AF.Sigmoid).then_inc(*sem.inc("p4a_w1", 1))

            @blk.vector
            def _(v, tc=tc):
                def _sv2(inst):
                    inst.then_inc(*sem.inc("vhz", 1))
                    sem.wait(v, "vhz", sem.count("vhz"))
                sem.wait(v, "p4a_w1", tc + 1)
                _sv2(v.tensor_scalar(w2c[:], w1c[:], -1.0, 1.0, ALU.mult, ALU.add))
                _sv2(v.scalar_tensor_tensor(cf1_sb[:, tc:tc + 1], ov1c[:], -1.0, ones_col[:],
                                            ALU.mult, ALU.add))
                _sv2(v.tensor_tensor(cf1_sb[:, tc:tc + 1], cf1_sb[:, tc:tc + 1], w1c[:], ALU.mult))
                _sv2(v.scalar_tensor_tensor(cf2_sb[:, tc:tc + 1], ov2c[:], -1.0, ones_col[:],
                                            ALU.mult, ALU.add))
                v.tensor_tensor(cf2_sb[:, tc:tc + 1], cf2_sb[:, tc:tc + 1], w2c[:], ALU.mult
                                ).then_inc(*sem.inc("p4a_cf", 1))

            @blk.gpsimd
            def _(g, tc=tc):
                if tc == 0:
                    sem.wait(g, "p4a_sent", 16)
                sem.wait(g, "p4a_idx", tc + 1)
                g.indirect_dma_start(
                    out=tokid_dram, out_offset=IndirectOffsetOnAxis(idx1_sb[:, tc:tc + 1], 0),
                    in_=tokid_sb[:, tc:tc + 1], in_offset=None,
                    bounds_check=rg_bc_row, oob_is_err=False,
                ).then_inc(*sem.inc("p4a_sc", 16))
                g.indirect_dma_start(
                    out=tokid_dram, out_offset=IndirectOffsetOnAxis(idx2_sb[:, tc:tc + 1], 0),
                    in_=tokid_sb[:, tc:tc + 1], in_offset=None,
                    bounds_check=rg_bc_row, oob_is_err=False,
                ).then_inc(*sem.inc("p4a_sc", 16))

        p4a.close()
        p4b = contextlib.ExitStack()
        such = [p4b.enter_context(nc.sbuf_tensor(f"such{i}", [128, 16, 128], BF16))
                for i in range(4)]
        sgev = p4b.enter_context(nc.sbuf_tensor("sgev", [128, 512], BF16))
        hhev = [p4b.enter_context(nc.sbuf_tensor(f"hhev{i}", [128, 512], BF16)) for i in range(2)]
        ps_g4 = [p4b.enter_context(nc.psum_tensor(f"ps_g4_{i}", [128, 512], F32)) for i in range(2)]
        ps_u4 = [p4b.enter_context(nc.psum_tensor(f"ps_u4_{i}", [128, 512], F32)) for i in range(2)]

        @blk.gpsimd
        def _(g):
            sem.wait(g, "p4a_cf", 16)
            for fc in range(44):
                for gu in range(2):
                    q = 2 * fc + gu
                    if q >= 4:
                        sem.wait(g, "p4b_mm", 4 * ((q - 4) // 2) + 4)
                    g.dma_start(out=such[q % 4][:],
                                in_=su_st[fc + 44 * gu].rearrange("(k p) m -> p k m", p=128)
                                ).then_inc(*sem.dinc("p4b_wld", q, 4))

        @blk.tensor
        def _(t):
            sem.wait(t, "p3_h2", 64)
            sem.wait(t, "p4a_rbup", 16)
            for fc in range(44):
                for tile in range(4):
                    n = 4 * fc + tile
                    for gu in range(2):
                        sem.dwait(t, "p4b_wld", 2 * fc + gu, 4)
                        ps = (ps_g4 if gu == 0 else ps_u4)[n % 2]
                        if n >= 2:
                            sem.wait(t, "p4b_hh", n - 1)
                        for k in range(NCH):
                            mm = t.matmul(ps[:], such[(2 * fc + gu) % 4][:, k, :],
                                          h2_sb[:, k, tile * TT:(tile + 1) * TT],
                                          start=(k == 0), stop=(k == NCH - 1))
                        if gu == 1:
                            mm.then_inc(*sem.inc("p4b_mm", 1))

        @blk.scalar
        def _(s):
            for fc in range(44):
                for tile in range(4):
                    n = 4 * fc + tile
                    sem.wait(s, "p4b_mm", n + 1)
                    if n >= 1:
                        sem.wait(s, "p4b_hh", n)
                    s.activation(out=sgev[:], in_=ps_g4[n % 2][:], func=AF.Silu
                                 ).then_inc(*sem.inc("p4b_sg", 1))

        @blk.vector
        def _(v):
            for fc in range(44):
                for tile in range(4):
                    n = 4 * fc + tile
                    sem.wait(v, "p4b_sg", n + 1)
                    if n >= 2:
                        sem.dwait(v, "p4b_hhst", n - 2, 2)
                    v.tensor_tensor(hhev[n % 2][:], sgev[:], ps_u4[n % 2][:], ALU.mult
                                    ).then_inc(*sem.inc("p4b_hh", 1))

        @blk.sync
        def _(sy):
            for fc in range(44):
                for tile in range(4):
                    n = 4 * fc + tile
                    sem.wait(sy, "p4b_hh", n + 1)
                    sy.dma_start(out=hh_dram[tile, fc, :, :], in_=hhev[n % 2][:]
                                 ).then_inc(*sem.dinc("p4b_hhst", n, 2))

        p4b.close()
        ph2.close()
        sem.retire(["p3_"], keep=("p3_h2tst0", "p3_h2tst1", "p3_h2tst2", "p3_h2tst3"))
        p4c = contextlib.ExitStack()
        ugch = [p4c.enter_context(nc.sbuf_tensor(f"ugch{i}", [128, 16, 128], BF16))
                for i in range(4)]
        tid4 = p4c.enter_context(nc.sbuf_tensor("tid4", [128, 4], I32))
        xe_tm = [p4c.enter_context(nc.sbuf_tensor(f"xe_tm{i}", [128, D], BF16)) for i in range(2)]
        xef = [p4c.enter_context(nc.sbuf_tensor(f"xef{i}", [128, 16, 512], BF16)) for i in range(2)]
        hh_e = [p4c.enter_context(nc.sbuf_tensor(f"hh_e{i}", [128, 11, 512], BF16)) for i in range(2)]
        sg4 = p4c.enter_context(nc.sbuf_tensor("sg4", [128, 512], BF16))
        wdsl = [p4c.enter_context(nc.sbuf_tensor(f"wdsl{i}", [128, 11, 512], BF16)) for i in range(2)]
        yev = [p4c.enter_context(nc.sbuf_tensor(f"yev{i}", [128, 512], F32)) for i in range(2)]
        ps_t4 = [p4c.enter_context(nc.psum_tensor(f"ps_t4_{i}", [128, 128], BF16)) for i in range(2)]
        ps_g5 = p4c.enter_context(nc.psum_tensor("ps_g5", [128, 512], F32))
        ps_u5 = p4c.enter_context(nc.psum_tensor("ps_u5", [128, 512], F32))
        ps_y4 = [p4c.enter_context(nc.psum_tensor(f"ps_y4_{i}", [128, 512], F32)) for i in range(2)]

        @blk.gpsimd
        def _(g):
            sem.wait(g, "p4b_mm", 176)
            sem.wait(g, "p4b_hh", 176)
            sem.dwait_all(g, "p4b_hhst", 2)
            sem.wait(g, "p4a_sc", 16 * 32)
            sem.dwait_all(g, "p3_h2tst", 4)
            for e in range(E):
                for gi in range(2):
                    u = 2 * e + gi
                    rows0 = e * C_LOC + gi * 512
                    if u >= 1:
                        sem.dwait(g, "p4c_gth", 4 * u - 1, 2)
                        sem.dwait(g, "p4c_gth", 4 * u - 2, 2)
                    g.dma_start(out=tid4[:],
                                in_=tokid_dram[rows0:rows0 + 512]
                                .rearrange("(c p) one -> p (c one)", p=128),
                                allow_slow_non_contiguous=True
                                ).then_inc(*sem.inc("p4c_tid", 16))
                    sem.wait(g, "p4c_tid", 16 * (u + 1))
                    for i in range(4):
                        gg = u * 4 + i
                        sem.wait(g, "p4c_ms", gg + 1)
                        g.indirect_dma_start(
                            out=xe_tm[gg % 2][:], out_offset=None,
                            in_=h2t_dram, in_offset=IndirectOffsetOnAxis(tid4[:, i:i + 1], 0),
                            bounds_check=rg_bc_tok, oob_is_err=False,
                        ).then_inc(*sem.dinc("p4c_gth", gg, 2))
                for fc in range(11):
                    for gu in range(2):
                        wl = (e * 11 + fc) * 2 + gu
                        if wl >= 4:
                            wlo = wl - 4
                            sem.wait(g, "p4c_ug", wlo - (wlo % 2) + 2)
                        g.dma_start(out=ugch[wl % 4][:],
                                    in_=ug_st[e, fc + 11 * gu].rearrange("(k p) m -> p k m", p=128)
                                    ).then_inc(*sem.dinc("p4c_ugld", wl, 4))
                for gi in range(2):
                    for ds in range(4):
                        nd = (2 * e + gi) * 4 + ds
                        if nd >= 2:
                            sem.wait(g, "p4c_ymm", 4 * (nd - 1))
                        g.dma_start(out=wdsl[nd % 2][:],
                                    in_=wd_st[e, :, ds * 512:(ds + 1) * 512]
                                    .rearrange("(ic p) n -> p ic n", p=128)
                                    ).then_inc(*sem.dinc("p4c_wdld", nd, 2))

        @blk.vector
        def _(v):
            sem.dwait_all(v, "p4b_hhst", 2)
            for e in range(E):
                for gi in range(2):
                    u = 2 * e + gi
                    for i in range(2):
                        gg = u * 4 + i
                        if gg >= 2:
                            sem.wait(v, "p4c_tr", 16 * (gg - 1))
                        v.memset(xe_tm[gg % 2][:], 0.0)
                        v.tensor_copy(xe_tm[gg % 2][:, 0:1], xe_tm[gg % 2][:, 0:1]
                                      ).then_inc(*sem.inc("p4c_ms", 1))
                    if e >= 1:
                        sem.wait(v, "p4c_ug", 22 * e)
                    for i in range(4):
                        gg = u * 4 + i
                        for c in range(NCH):
                            nt = gg * NCH + c
                            sem.wait(v, "p4c_tr", nt + 1)
                            v.tensor_copy(xef[gi][:, c, i * 128:(i + 1) * 128], ps_t4[nt % 2][:]
                                          ).then_inc(*sem.inc("p4c_trev", 1))
                        if i + 2 <= 3:
                            gg2 = u * 4 + i + 2
                            sem.wait(v, "p4c_tr", 16 * (gg2 - 1))
                            v.memset(xe_tm[gg2 % 2][:], 0.0)
                            v.tensor_copy(xe_tm[gg2 % 2][:, 0:1], xe_tm[gg2 % 2][:, 0:1]
                                          ).then_inc(*sem.inc("p4c_ms", 1))
                if e >= 1:
                    sem.wait(v, "p4c_ymm", 32 * e)
                for fc in range(11):
                    for gi in range(2):
                        nf = (e * 11 + fc) * 2 + gi
                        sem.wait(v, "p4c_sg", nf + 1)
                        v.tensor_tensor(hh_e[gi][:, fc, :], sg4[:], ps_u5[:], ALU.mult
                                        ).then_inc(*sem.inc("p4c_hh", 1))
                for gi in range(2):
                    for ds in range(4):
                        for r in range(4):
                            ny = ((2 * e + gi) * 4 + ds) * 4 + r
                            sem.wait(v, "p4c_ymm", ny + 1)
                            if ny >= 2:
                                sem.dwait(v, "p4c_yst", ny - 2, 2)
                            v.tensor_copy(yev[ny % 2][:], ps_y4[ny % 2][:]
                                          ).then_inc(*sem.inc("p4c_yev", 1))

        @blk.tensor
        def _(t):
            sem.wait(t, "p4b_hh", 176)
            for e in range(E):
                for gi in range(2):
                    u = 2 * e + gi
                    for i in range(4):
                        gg = u * 4 + i
                        sem.dwait(t, "p4c_gth", gg, 2)
                        for c in range(NCH):
                            nt = gg * NCH + c
                            if nt >= 2:
                                sem.wait(t, "p4c_trev", nt - 1)
                            t.transpose(ps_t4[nt % 2][:], xe_tm[gg % 2][:, c * 128:(c + 1) * 128],
                                        identb_sb[:]).then_inc(*sem.inc("p4c_tr", 1))
                sem.wait(t, "p4c_trev", 128 * (e + 1))
                for fc in range(11):
                    sem.dwait(t, "p4c_ugld", (e * 11 + fc) * 2 + 0, 4)
                    sem.dwait(t, "p4c_ugld", (e * 11 + fc) * 2 + 1, 4)
                    for gi in range(2):
                        nf = (e * 11 + fc) * 2 + gi
                        if nf >= 1:
                            sem.wait(t, "p4c_hh", nf)
                        for k in range(NCH):
                            t.matmul(ps_g5[:], ugch[((e * 11 + fc) * 2) % 4][:, k, :],
                                     xef[gi][:, k, :], start=(k == 0), stop=(k == NCH - 1))
                        for k in range(NCH):
                            mm = t.matmul(ps_u5[:], ugch[((e * 11 + fc) * 2 + 1) % 4][:, k, :],
                                          xef[gi][:, k, :], start=(k == 0), stop=(k == NCH - 1))
                        mm.then_inc(*sem.inc("p4c_ug", 1))
                sem.wait(t, "p4c_hh", 22 * (e + 1))
                for gi in range(2):
                    for ds in range(4):
                        nd = (2 * e + gi) * 4 + ds
                        sem.dwait(t, "p4c_wdld", nd, 2)
                        for r in range(4):
                            ny = nd * 4 + r
                            if ny >= 2:
                                sem.wait(t, "p4c_yev", ny - 1)
                            for ic in range(11):
                                mm = t.matmul(ps_y4[ny % 2][:],
                                              hh_e[gi][:, ic, r * 128:(r + 1) * 128],
                                              wdsl[nd % 2][:, ic, :],
                                              start=(ic == 0), stop=(ic == 10))
                            mm.then_inc(*sem.inc("p4c_ymm", 1))

        @blk.scalar
        def _(s):
            for e in range(E):
                for fc in range(11):
                    for gi in range(2):
                        nf = (e * 11 + fc) * 2 + gi
                        sem.wait(s, "p4c_ug", nf + 1)
                        if nf >= 1:
                            sem.wait(s, "p4c_hh", nf)
                        s.activation(out=sg4[:], in_=ps_g5[:], func=AF.Silu
                                     ).then_inc(*sem.inc("p4c_sg", 1))

        @blk.sync
        def _(sy):
            for e in range(E):
                for gi in range(2):
                    rows0 = e * C_LOC + gi * 512
                    for ds in range(4):
                        for r in range(4):
                            ny = ((2 * e + gi) * 4 + ds) * 4 + r
                            sem.wait(sy, "p4c_yev", ny + 1)
                            sy.dma_start(out=ybuf_dram[rows0 + r * 128: rows0 + (r + 1) * 128,
                                                       ds * 512:(ds + 1) * 512],
                                         in_=yev[ny % 2][:]).then_inc(*sem.dinc("p4c_yst", ny, 2))

        p4c.close()
        sem.retire(["p4a_", "p4b_wld", "p4b_sg", "p4b_mm", "p4b_hh", "p3_h2tst"],
                   keep=("p4b_hhst0", "p4b_hhst1"))
        p4d = contextlib.ExitStack()
        wsd4 = p4d.enter_context(nc.sbuf_tensor("wsd4", [128, 44, 512], BF16))
        hht = [p4d.enter_context(nc.sbuf_tensor(f"hht{i}", [128, 44, 512], BF16)) for i in range(2)]
        yse = [p4d.enter_context(nc.sbuf_tensor(f"yse{i}", [128, 512], F32)) for i in range(2)]
        ps_y5 = [p4d.enter_context(nc.psum_tensor(f"ps_y5_{i}", [128, 512], F32)) for i in range(2)]

        @blk.gpsimd
        def _(g):
            sem.wait(g, "p4c_ymm", 256)
            for ds in range(4):
                if ds >= 1:
                    sem.wait(g, "p4d_ymm", 16 * ds)
                g.dma_start(out=wsd4[:], in_=wsd_st[ds].rearrange("(ic p) n -> p ic n", p=128)
                            ).then_inc(*sem.inc("p4d_wld", 16))

        @blk.sync
        def _(sy):
            sem.dwait_all(sy, "p4b_hhst", 2)
            sem.wait(sy, "p4c_ymm", 256)
            for ds in range(4):
                for tile in range(4):
                    n = 4 * ds + tile
                    if n >= 2:
                        sem.wait(sy, "p4d_ymm", 4 * (n - 1))
                    sy.dma_start(out=hht[n % 2][:], in_=hh_dram[tile].rearrange("ic p n -> p ic n")
                                 ).then_inc(*sem.dinc("p4d_hld", n, 2))
                    for r in range(4):
                        ny = n * 4 + r
                        sem.wait(sy, "p4d_yev", ny + 1)
                        sy.dma_start(out=ysh_dram[tile * TT + r * 128: tile * TT + (r + 1) * 128,
                                                  ds * 512:(ds + 1) * 512],
                                     in_=yse[ny % 2][:]).then_inc(*sem.dinc("p4d_yst", ny, 2))

        @blk.tensor
        def _(t):
            sem.wait(t, "p4c_yev", 256)
            for ds in range(4):
                sem.wait(t, "p4d_wld", 16 * (ds + 1))
                for tile in range(4):
                    n = 4 * ds + tile
                    sem.dwait(t, "p4d_hld", n, 2)
                    for r in range(4):
                        ny = n * 4 + r
                        if ny >= 2:
                            sem.wait(t, "p4d_yev", ny - 1)
                        for ic in range(44):
                            mm = t.matmul(ps_y5[ny % 2][:], hht[n % 2][:, ic, r * 128:(r + 1) * 128],
                                          wsd4[:, ic, :], start=(ic == 0), stop=(ic == 43))
                        mm.then_inc(*sem.inc("p4d_ymm", 1))

        @blk.vector
        def _(v):
            sem.dwait_all(v, "p4c_yst", 2)
            for ds in range(4):
                for tile in range(4):
                    for r in range(4):
                        ny = (4 * ds + tile) * 4 + r
                        sem.wait(v, "p4d_ymm", ny + 1)
                        if ny >= 2:
                            sem.dwait(v, "p4d_yst", ny - 2, 2)
                        v.tensor_copy(yse[ny % 2][:], ps_y5[ny % 2][:]).then_inc(*sem.inc("p4d_yev", 1))

        p4d.close()
        p5 = contextlib.ExitStack()
        ysr = [p5.enter_context(nc.sbuf_tensor(f"ysr{i}", [128, D], F32)) for i in range(2)]
        g1s = [p5.enter_context(nc.sbuf_tensor(f"g1s{i}", [128, D], F32)) for i in range(2)]
        g2s = [p5.enter_context(nc.sbuf_tensor(f"g2s{i}", [128, D], F32)) for i in range(2)]
        ytot = [p5.enter_context(nc.sbuf_tensor(f"ytot{i}", [128, D], F32)) for i in range(2)]
        xpc = [p5.enter_context(nc.sbuf_tensor(f"xpc{i}", [128, 128], F32)) for i in range(2)]
        outc = [p5.enter_context(nc.sbuf_tensor(f"outc{i}", [128, 128], F32)) for i in range(2)]
        ps_t5 = [p5.enter_context(nc.psum_tensor(f"ps_t5_{i}", [128, 128], F32)) for i in range(2)]

        @blk.sync
        def _(sy):
            sem.dwait_all(sy, "p4d_yst", 2)
            for tc in range(16):
                if tc >= 2:
                    sem.wait(sy, "p5_yt", tc - 1)
                sy.dma_start(out=ysr[tc % 2][:], in_=ysh_dram[tc * 128:(tc + 1) * 128, :]
                             ).then_inc(*sem.dinc("p5_yld", tc, 2))
                for fc in range(NCH):
                    nt = tc * NCH + fc
                    if nt >= 2:
                        sem.wait(sy, "p5_oev", nt - 1)
                    sy.dma_start(out=xpc[nt % 2][:], in_=xp_dram[fc, :, tc * 128:(tc + 1) * 128]
                                 ).then_inc(*sem.dinc("p5_xld", nt, 2))
                    if nt >= 1:
                        sem.wait(sy, "p5_oev", nt)
                        tcp, fcp = (nt - 1) // NCH, (nt - 1) % NCH
                        sy.dma_start(out=y_out[fcp, :, tcp * 128:(tcp + 1) * 128],
                                     in_=outc[(nt - 1) % 2][:]).then_inc(*sem.dinc("p5_ost", nt - 1, 2))
            sem.wait(sy, "p5_oev", 256)
            sy.dma_start(out=y_out[15, :, 15 * 128:16 * 128], in_=outc[255 % 2][:]
                         ).then_inc(*sem.dinc("p5_ost", 255, 2))
            sem.dwait_all(sy, "p5_ost", 2)

        @blk.gpsimd
        def _(g):
            sem.dwait_all(g, "p4c_yst", 2)
            sem.dwait_all(g, "p4d_yst", 2)
            for tc in range(16):
                sem.wait(g, "p5_ms", 2 * (tc + 1))
                g.indirect_dma_start(
                    out=g1s[tc % 2][:], out_offset=None,
                    in_=ybuf_dram, in_offset=IndirectOffsetOnAxis(idx1_sb[:, tc:tc + 1], 0),
                    bounds_check=rg_bc_row, oob_is_err=False,
                ).then_inc(*sem.dinc("p5_g1", tc, 2))
                g.indirect_dma_start(
                    out=g2s[tc % 2][:], out_offset=None,
                    in_=ybuf_dram, in_offset=IndirectOffsetOnAxis(idx2_sb[:, tc:tc + 1], 0),
                    bounds_check=rg_bc_row, oob_is_err=False,
                ).then_inc(*sem.dinc("p5_g2", tc, 2))

        @blk.vector
        def _(v):
            sem.wait(v, "p4d_yev", 64)
            for tc in range(16):
                if tc >= 2:
                    sem.wait(v, "p5_yt", tc - 1)
                    sem.wait(v, "p5_tr", 16 * (tc - 1))
                v.memset(g1s[tc % 2][:], 0.0)
                v.tensor_copy(g1s[tc % 2][:, 0:1], g1s[tc % 2][:, 0:1]).then_inc(*sem.inc("p5_ms", 1))
                v.memset(g2s[tc % 2][:], 0.0)
                v.tensor_copy(g2s[tc % 2][:, 0:1], g2s[tc % 2][:, 0:1]).then_inc(*sem.inc("p5_ms", 1))
                sem.dwait(v, "p5_g1", tc, 2)
                sem.dwait(v, "p5_g2", tc, 2)
                sem.dwait(v, "p5_yld", tc, 2)
                v.tensor_scalar_mul(g1s[tc % 2][:], g1s[tc % 2][:], cf1_sb[:, tc:tc + 1])
                v.scalar_tensor_tensor(ytot[tc % 2][:], g2s[tc % 2][:], cf2_sb[:, tc:tc + 1],
                                       g1s[tc % 2][:], ALU.mult, ALU.add)
                v.tensor_tensor(ytot[tc % 2][:], ytot[tc % 2][:], ysr[tc % 2][:], ALU.add
                                ).then_inc(*sem.inc("p5_yt", 1))
                for fc in range(NCH):
                    nt = tc * NCH + fc
                    sem.wait(v, "p5_tr", nt + 1)
                    sem.dwait(v, "p5_xld", nt, 2)
                    if nt >= 2:
                        sem.dwait(v, "p5_ost", nt - 2, 2)
                    v.scalar_tensor_tensor(outc[nt % 2][:], ps_t5[nt % 2][:],
                                           mod_sb[:, 80 + fc:81 + fc], xpc[nt % 2][:],
                                           ALU.mult, ALU.add).then_inc(*sem.inc("p5_oev", 1))

        @blk.tensor
        def _(t):
            sem.wait(t, "p4d_yev", 64)
            for tc in range(16):
                sem.wait(t, "p5_yt", tc + 1)
                for fc in range(NCH):
                    nt = tc * NCH + fc
                    if nt >= 2:
                        sem.wait(t, "p5_oev", nt - 1)
                    t.transpose(ps_t5[nt % 2][:], ytot[tc % 2][:, fc * 128:(fc + 1) * 128],
                                identf_sb[:]).then_inc(*sem.inc("p5_tr", 1))



        p5.close()
    return nc


def _stage_weights(inputs):
    f32 = np.float32

    def ca(a):
        return np.ascontiguousarray(a, dtype=f32)

    w_qkv = inputs["L0_attn_w_qkv"]
    wq, wk, wv = w_qkv[0:2048], w_qkv[2048:4096], w_qkv[4096:6144]
    st = {}
    st["wmod_st"] = ca(np.asarray(inputs["L0_wmod"]).reshape(NCH, 128, 6 * D))
    st["slope_row"] = ca(np.asarray(inputs["L0_attn_slope_rate"]).reshape(1, H))
    st["normw_st"] = ca(np.asarray(inputs["L0_attn_norm_w"]).reshape(NCH, 128).T)
    st["ident_f"] = np.eye(128, dtype=f32)
    st["su_mat"] = np.triu(np.ones((128, 128), f32), k=1)
    st["wq_st"] = ca(np.asarray(wq).T.reshape(D, 16, 128).transpose(1, 0, 2))
    st["wk_st"] = ca(np.asarray(wk).T.reshape(D, 16, 128).transpose(1, 0, 2))
    st["wg_st"] = ca(np.asarray(inputs["L0_attn_w_gate"]).T.reshape(D, 16, 128).transpose(1, 0, 2))
    st["wo_st"] = ca(np.asarray(inputs["L0_attn_w_out"]).T.reshape(D, 16, 128).transpose(1, 0, 2))
    st["wv_tm"] = ca(np.asarray(wv).T.reshape(NCH, 128, D))
    st["wk_tm"] = ca(np.asarray(wk).T.reshape(NCH, 128, D))
    st["router_st"] = ca(np.asarray(inputs["L0_ffn_router"]).reshape(NCH, 128, E).transpose(1, 0, 2))
    st["ug_st"] = ca(np.asarray(inputs["L0_ffn_w_up_gate"]).reshape(E, D, 22, 128).transpose(0, 2, 1, 3))
    st["wd_st"] = ca(np.asarray(inputs["L0_ffn_w_down"]))
    st["su_st"] = ca(np.asarray(inputs["L0_ffn_shared_w_up_gate"]).reshape(D, 88, 128).transpose(1, 0, 2))
    st["wsd_st"] = ca(np.asarray(inputs["L0_ffn_shared_w_down"]).reshape(J_SH, 4, 512).transpose(1, 0, 2))
    return st


def kernel(**inputs):
    nc = build_program()
    shared = _stage_weights(inputs)
    x = np.ascontiguousarray(np.asarray(inputs["L0_attn_hidden_states"]), dtype=np.float32)
    c_all = np.ascontiguousarray(np.asarray(inputs["L0_c"]), dtype=np.float32)

    in_maps = []
    for core in range(8):
        b, hf = core // 2, core % 2
        m = dict(shared)
        m["x_own"] = np.ascontiguousarray(x[b, hf * T:(hf + 1) * T, :].T.reshape(NCH, 128, T))
        m["x_pri"] = np.ascontiguousarray(x[b, 0:T, :].T.reshape(NCH, 128, T))
        perm = [b] + [i for i in range(4) if i != b]
        m["c_t"] = np.ascontiguousarray(c_all[perm].T)
        m["seed_mask"] = np.full((128, 1), 1.0 if hf == 1 else 0.0, np.float32)
        in_maps.append(m)

    res = run_bass_kernel_spmd(nc, in_maps, core_ids=list(range(8)))

    out = np.empty((B, N, D), np.float32)
    for core in range(8):
        b, hf = core // 2, core % 2
        r = res.results[core]["y_out"].reshape(D, T)
        out[b, hf * T:(hf + 1) * T, :] = r.T
    return out
